# version 26
# baseline (speedup 1.0000x reference)
import math
import numpy as np
import concourse.bass as bass
import concourse.mybir as mybir
from concourse.bass_utils import run_bass_kernel_spmd

F32 = mybir.dt.float32
AF = mybir.ActivationFunctionType
ALU = mybir.AluOpType
AX = mybir.AxisListType

D = 1024
NCH = 8
NH = 16
NKV = 4
HD = 64
B, L = 4, 4096
CORE_T = 2048
HALO = 16
TILE = 512
NT = CORE_T // TILE
TT = TILE + HALO
SUBTILES = [(0, 112), (112, 112), (224, 112), (336, 112), (448, 64)]
VGROUPS = [(0, 128), (112, 128), (224, 128), (336, 128), (448, 80)]
NEG = -1e30
SHIFT = 8.0


def _slopes():
    return np.array([2.0 ** (-0.5 * (h + 1)) for h in range(NH)], dtype=np.float64)


def _mask_tile(qlen, first):
    kw = qlen + 16
    sl = _slopes()
    m = np.full((qlen, NH, kw), NEG, dtype=np.float64)
    for i in range(qlen):
        for j in range(i, min(kw, i + 17)):
            if first and j < 16:
                continue
            m[i, :, j] = sl * (j - i - 16) - SHIFT
    m = m[:, [2 * (s % 8) + s // 8 for s in range(NH)], :]
    return m.astype(np.float32).reshape(qlen, NH * kw)


def _consts():
    c = {}
    c["ident"] = np.eye(128, dtype=np.float32)
    c["logm"] = _mask_tile(112, False)
    c["logmf"] = _mask_tile(112, True)
    c["logm64"] = _mask_tile(64, False)
    selq = np.zeros((128, NCH * NH), dtype=np.float32)
    selqT = np.zeros((NH, NCH * 128), dtype=np.float32)
    for ci in range(NCH):
        for p in range(128):
            h = 2 * ci + (1 if p >= 64 else 0)
            selq[p, 16 * ci + h] = 1.0 / 64.0
            selqT[h, 128 * ci + p] = 1.0
    c["selq"], c["selqT"] = selq, selqT
    selk = np.zeros((128, 2 * NKV), dtype=np.float32)
    selkT = np.zeros((NKV, 2 * 128), dtype=np.float32)
    for ci in range(2):
        for p in range(128):
            kv = 2 * ci + (1 if p >= 64 else 0)
            selk[p, 4 * ci + kv] = 1.0 / 64.0
            selkT[kv, 128 * ci + p] = 1.0
    c["selk"], c["selkT"] = selk, selkT
    return c


_DBG_STOP = [None]


class _StopBuild(Exception):
    pass


def _chk(tt, tag):
    st = _DBG_STOP[0]
    if st is None:
        return
    stile, stag = st
    if tt > stile or (tt == stile and tag == stag):
        raise _StopBuild()


class _Sem:
    def __init__(self, nc, name):
        self.h = nc.semaphore(name).__enter__()
        self.n = 0

    def inc(self, inst, k=1):
        inst.then_inc(self.h, k)
        self.n += k


class _DefInst:
    __slots__ = ("incs",)

    def __init__(self):
        self.incs = []

    def then_inc(self, sem, k):
        self.incs.append((sem, k))
        return self


def _ap_span(ap):
    try:
        t = ap.tensor.name
        pairs = [list(p) for p in ap.ap]
        bp = ap.base_partition
        if callable(bp):
            bp = bp()
        bp = bp or 0
        np_ = pairs[0][1] if pairs else 128
        lo = ap.offset
        hi = ap.offset
        for st, cnt in pairs[1:]:
            if st > 0:
                hi += st * (cnt - 1)
            elif st < 0:
                lo += st * (cnt - 1)
        return (t, bp, bp + np_, lo, hi + 1)
    except Exception:
        return None


def _overlap(s1, s2):
    return (s1[0] == s2[0] and s1[1] < s2[2] and s2[1] < s1[2]
            and s1[3] < s2[4] and s2[3] < s1[4])


class _EngRec:

    _HAZARD = ("vector", "scalar", "gpsimd")

    def __init__(self, name):
        self._name = name
        self._ops = []
        self._wr = []
        self._rd = []

    def __getattr__(self, attr):
        if attr.startswith("_"):
            raise AttributeError(attr)
        rec = self

        def call(*a, **kw):
            if rec._name in _EngRec._HAZARD and attr not in ("wait_ge", "drain"):
                wsp, rsp = [], []
                args = list(a)
                for i, arg in enumerate(args):
                    sp = _ap_span(arg) if hasattr(arg, "tensor") else None
                    if sp is None:
                        continue
                    (wsp if i == 0 else rsp).append(sp)
                for k2, arg in kw.items():
                    sp = _ap_span(arg) if hasattr(arg, "tensor") else None
                    if sp is None:
                        continue
                    (wsp if k2 == "accum_out" else rsp).append(sp)
                conflict = any(_overlap(s, w) for s in wsp + rsp for w in rec._wr)                     or any(_overlap(s, r) for s in wsp for r in rec._rd)
                if conflict:
                    rec._ops.append(("drain", (), {}, _DefInst()))
                    rec._wr, rec._rd = [], []
                rec._wr.extend(wsp)
                rec._rd.extend(rsp)
            di = _DefInst()
            rec._ops.append((attr, a, kw, di))
            return di
        return call

    def _flush(self, eng):
        for attr, a, kw, di in self._ops:
            inst = getattr(eng, attr)(*a, **kw)
            for sem, k in di.incs:
                inst.then_inc(sem, k)


def build_program():
    nc = bass.Bass("TRN2", target_bir_lowering=False)
    xin = nc.dram_tensor("xin", [D, CORE_T + HALO], F32, kind="ExternalInput").ap()
    yout = nc.dram_tensor("yout", [D, CORE_T], F32, kind="ExternalOutput").ap()
    w_wq = nc.dram_tensor("w_wq", [D, D], F32, kind="ExternalInput").ap()
    w_wk = nc.dram_tensor("w_wk", [D, 256], F32, kind="ExternalInput").ap()
    w_wv = nc.dram_tensor("w_wv", [D, 256], F32, kind="ExternalInput").ap()
    w_wo = nc.dram_tensor("w_wo", [D, D], F32, kind="ExternalInput").ap()
    w_wr = nc.dram_tensor("w_wr", [D, D], F32, kind="ExternalInput").ap()
    w_wkf = nc.dram_tensor("w_wkf", [D, D], F32, kind="ExternalInput").ap()
    w_wvf = nc.dram_tensor("w_wvf", [D, D], F32, kind="ExternalInput").ap()
    c_ident = nc.dram_tensor("c_ident", [128, 128], F32, kind="ExternalInput").ap()
    c_logm = nc.dram_tensor("c_logm", [112, NH * 128], F32, kind="ExternalInput").ap()
    c_logmf = nc.dram_tensor("c_logmf", [112, NH * 128], F32, kind="ExternalInput").ap()
    c_logm64 = nc.dram_tensor("c_logm64", [64, NH * 80], F32, kind="ExternalInput").ap()
    c_selq = nc.dram_tensor("c_selq", [128, NCH * NH], F32, kind="ExternalInput").ap()
    c_selqT = nc.dram_tensor("c_selqT", [NH, NCH * 128], F32, kind="ExternalInput").ap()
    c_selk = nc.dram_tensor("c_selk", [128, 2 * NKV], F32, kind="ExternalInput").ap()
    c_selkT = nc.dram_tensor("c_selkT", [NKV, 2 * 128], F32, kind="ExternalInput").ap()
    c_qnw = nc.dram_tensor("c_qnw", [128, 1], F32, kind="ExternalInput").ap()
    c_knw = nc.dram_tensor("c_knw", [128, 1], F32, kind="ExternalInput").ap()
    c_mur = nc.dram_tensor("c_mur", [128, NCH], F32, kind="ExternalInput").ap()
    c_muk = nc.dram_tensor("c_muk", [128, NCH], F32, kind="ExternalInput").ap()
    c_scal = nc.dram_tensor("c_scal", [128, 8], F32, kind="ExternalInput").ap()

    sb = lambda name, shape: nc.alloc_sbuf_tensor(name, shape, F32).ap()
    pm = lambda name, shape: nc.alloc_psum_tensor(name, shape, F32).ap()
    xbuf = sb("xbuf", [128, NCH * TT])
    xnbuf = sb("xnbuf", [128, NCH * TT])
    scr = sb("scr", [128, NCH * TT])
    qbuf = sb("qbuf", [128, NCH * TILE])
    xmix = sb("xmix", [128, NCH * TILE])
    kbuf = sb("kbuf", [128, 2 * TT])
    vbuf = sb("vbuf", [128, 5 * 256])
    pbuf = sb("pbuf", [128, NH * 128])
    ptbuf = sb("ptbuf", [128, NH * 112])
    asub = sb("asub", [128, NCH * 112])
    lbuf = sb("lbuf", [128, 2 * NH])
    rbuf2 = sb("rbuf2", [128, 2 * 512])
    qrow = sb("qrow", [NH, 2 * TILE])
    krow = sb("krow", [NKV, 2 * TT])
    lrow = sb("lrow", [1, 4 * TT])
    wslot = sb("wslot", [128, 3 * 4096])
    ident = sb("ident", [128, 128])
    logm = sb("logm", [112, NH * 128])
    logmf = sb("logmf", [112, NH * 128])
    logm64 = sb("logm64", [64, NH * 80])
    selq = sb("selq", [128, NCH * NH])
    selqT = sb("selqT", [NH, NCH * 128])
    selk = sb("selk", [128, 2 * NKV])
    selkT = sb("selkT", [NKV, 2 * 128])
    qnw = sb("qnw", [128, 1])
    knw = sb("knw", [128, 1])
    mur = sb("mur", [128, NCH])
    muk = sb("muk", [128, NCH])
    scal = sb("scal", [128, 8])
    ones1 = sb("ones1", [1, 128])
    invD = sb("invD", [128, 1])
    psS = pm("psS", [128, 2048])
    psPV = pm("psPV", [128, 1024])
    psA = pm("psA", [128, 512])
    psB = pm("psB", [128, 512])
    with nc.Block() as block:
        s_c = _Sem(nc, "s_c")
        s_ws = [_Sem(nc, "s_ws0"), _Sem(nc, "s_ws1"), _Sem(nc, "s_ws2")]
        s_x = _Sem(nc, "s_x")
        s_y = _Sem(nc, "s_y")
        s_t = _Sem(nc, "s_t")
        s_v = _Sem(nc, "s_v")
        s_s = _Sem(nc, "s_s")
        s_g = _Sem(nc, "s_g")

        T = _EngRec("tensor")
        V = _EngRec("vector")
        S = _EngRec("scalar")
        G = _EngRec("gpsimd")
        SY = _EngRec("sync")

        C_SCALE = lambda p: scal[0:p, 0:1]
        C_EPSLN = lambda p: scal[0:p, 1:2]
        C_EPSRM = lambda p: scal[0:p, 2:3]
        C_NEG1 = lambda p: scal[0:p, 3:4]
        C_ONE = lambda p: scal[0:p, 4:5]
        C_ZERO = lambda p: scal[0:p, 5:6]

        for dst, src in [(scal, c_scal), (ident, c_ident), (logm, c_logm),
                         (logmf, c_logmf), (logm64, c_logm64), (selq, c_selq),
                         (selqT, c_selqT), (selk, c_selk), (selkT, c_selkT),
                         (qnw, c_qnw), (knw, c_knw), (mur, c_mur), (muk, c_muk)]:
            s_c.inc(SY.dma_start(out=dst[:, :], in_=src[:, :]), 16)
        CONST_W = s_c.n
        V.wait_ge(s_c.h, CONST_W)
        V.memset(ones1[:, :], 1.0)
        s_v.inc(V.memset(invD[:, :], 1.0 / float(D)))
        INIT_V = s_v.n

        wq3 = w_wq.rearrange("(c p) o -> p c o", p=128)
        wo3 = w_wo.rearrange("(c p) o -> p c o", p=128)
        wr3 = w_wr.rearrange("(c p) o -> p c o", p=128)
        wkf3 = w_wkf.rearrange("(c p) o -> p c o", p=128)
        wvf3 = w_wvf.rearrange("(c p) o -> p c o", p=128)
        wk3 = w_wk.rearrange("(c p) o -> p c o", p=128)
        wv3 = w_wv.rearrange("(c p) o -> p c o", p=128)
        xin3 = xin.rearrange("(c p) t -> p c t", p=128)
        yout3 = yout.rearrange("(c p) t -> p c t", p=128)

        wload_idx = [0]
        wslot_free = [None, None, None]

        def wslot_view(idx):
            sl = idx % 3
            return wslot[:, 4096 * sl: 4096 * (sl + 1)]

        def load_weight_half(src3, half, kvpair=False):
            idx = wload_idx[0]
            sl = idx % 3
            if wslot_free[sl] is not None:
                sem, cnt = wslot_free[sl]
                SY.wait_ge(sem, cnt)
            dst = wslot_view(idx)
            if kvpair:
                s_ws[sl].inc(SY.dma_start(
                    out=dst.rearrange("p (c o) -> p c o", o=512)[:, :, 0:256],
                    in_=wk3[:, :, :]), 16)
                s_ws[sl].inc(SY.dma_start(
                    out=dst.rearrange("p (c o) -> p c o", o=512)[:, :, 256:512],
                    in_=wv3[:, :, :]), 16)
            else:
                s_ws[sl].inc(SY.dma_start(
                    out=dst.rearrange("p (c o) -> p c o", o=512),
                    in_=src3[:, :, 512 * half: 512 * (half + 1)]), 16)
            wload_idx[0] += 1
            return (idx, s_ws[sl].n)

        wkv_info = load_weight_half(None, 0, kvpair=True)
        wq_info = [load_weight_half(wq3, 0), None]

        bank_free = {"A": None, "B": None, "S0": None, "S1": None}
        ROT4 = ["A", "B", "S0", "S1"]

        def bank_ps(bank):
            if bank == "A":
                return psA
            if bank == "B":
                return psB
            if bank == "S0":
                return psS[:, 0:512]
            return psS[:, 512:1024]

        def mm_gate(bank):
            if bank_free[bank] is not None:
                sem, cnt = bank_free[bank]
                T.wait_ge(sem, cnt)

        X_READY = None
        KMM_LAST = None
        VMM_LAST = None
        GADD_LAST = None
        PSS_GATE = None
        PSPV_GATE = None
        VEVAC_LAST = None
        try:
          for tt in range(NT):
            if tt == 0:
                s_x.inc(SY.dma_start(
                    out=xbuf.rearrange("p (c t) -> p c t", t=TT),
                    in_=xin3[:, :, 0:TT]), 16)
                X_READY = s_x.n

            V.wait_ge(s_x.h, X_READY)
            if tt == 0:
                V.wait_ge(s_c.h, CONST_W)
            if KMM_LAST is not None:
                V.wait_ge(s_t.h, KMM_LAST)
            s_v.inc(V.tensor_tensor(scr[:, 0:NCH * TT], xbuf[:, 0:NCH * TT],
                                    xbuf[:, 0:NCH * TT], op=ALU.mult))
            XSQ_V = s_v.n
            T.wait_ge(s_x.h, X_READY)
            T.wait_ge(s_v.h, INIT_V)
            if PSPV_GATE is not None:
                T.wait_ge(PSPV_GATE[0], PSPV_GATE[1])
            mm_gate("A")
            for ci in range(NCH):
                T.matmul(psA[0:1, 0:512], invD[:, 0:1],
                         xbuf[:, TT * ci: TT * ci + 512],
                         start=(ci == 0), stop=(ci == NCH - 1))
            for ci in range(NCH):
                T.matmul(psPV[0:1, 0:16], invD[:, 0:1],
                         xbuf[:, TT * ci + 512: TT * ci + TT],
                         start=(ci == 0), stop=(ci == NCH - 1))
            T.wait_ge(s_v.h, XSQ_V)
            mm_gate("B")
            for ci in range(NCH):
                T.matmul(psB[0:1, 0:512], invD[:, 0:1],
                         scr[:, TT * ci: TT * ci + 512],
                         start=(ci == 0), stop=(ci == NCH - 1))
            for ci in range(NCH):
                inst = T.matmul(psPV[0:1, 512:528], invD[:, 0:1],
                                scr[:, TT * ci + 512: TT * ci + TT],
                                start=(ci == 0), stop=(ci == NCH - 1))
            s_t.inc(inst)
            STATS_T = s_t.n
            r0 = lrow[0:1, 0:TT]
            r1 = lrow[0:1, TT:2 * TT]
            r2 = lrow[0:1, 2 * TT:3 * TT]
            r3 = lrow[0:1, 3 * TT:4 * TT]
            V.wait_ge(s_t.h, STATS_T)
            V.tensor_copy(lrow[0:1, 0:512], psA[0:1, 0:512])
            V.tensor_copy(lrow[0:1, 512:TT], psPV[0:1, 0:16])
            V.tensor_copy(lrow[0:1, TT:TT + 512], psB[0:1, 0:512])
            s_v.inc(V.tensor_copy(lrow[0:1, TT + 512:2 * TT], psPV[0:1, 512:528]))
            bank_free["A"] = (s_v.h, s_v.n)
            bank_free["B"] = (s_v.h, s_v.n)
            psPV_rows_done = (s_v.h, s_v.n)
            V.tensor_tensor(r2, r0, r0, op=ALU.mult)
            V.tensor_tensor(r1, r1, r2, op=ALU.subtract)
            V.tensor_scalar_add(r1, r1, C_EPSLN(1))
            s_v.inc(V.reciprocal(r2, r1))
            S.wait_ge(s_v.h, s_v.n)
            s_s.inc(S.activation(r3, r2, AF.Sqrt))
            V.wait_ge(s_s.h, s_s.n)
            s_v.inc(V.scalar_tensor_tensor(r2, r0, C_NEG1(1), r3,
                                           op0=ALU.mult, op1=ALU.mult))
            LNROWS_V = s_v.n
            T.wait_ge(s_v.h, LNROWS_V)
            if PSS_GATE is not None:
                T.wait_ge(PSS_GATE[0], PSS_GATE[1])
            mm_gate("S0")
            mm_gate("S1")
            T.matmul(psS[:, 0:512], ones1[0:1, :], r3[0:1, 0:512], start=True, stop=True)
            T.matmul(psS[:, 512:TT], ones1[0:1, :], r3[0:1, 512:TT], start=True, stop=True)
            T.matmul(psS[:, 1024:1536], ones1[0:1, :], r2[0:1, 0:512], start=True, stop=True)
            inst = T.matmul(psS[:, 1536:1024 + TT], ones1[0:1, :], r2[0:1, 512:TT],
                            start=True, stop=True)
            s_t.inc(inst)
            LNB_T = s_t.n
            V.wait_ge(s_t.h, LNB_T)
            if tt > 0:
                V.wait_ge(s_y.h, 16 * tt)
            xn_chunk_v = []
            for ci in range(NCH):
                xs = xbuf[:, TT * ci: TT * ci + TT]
                xn_ = xnbuf[:, TT * ci: TT * ci + TT]
                V.tensor_tensor(xn_, xs, psS[:, 0:TT], op=ALU.mult)
                s_v.inc(V.tensor_tensor(xn_, xn_, psS[:, 1024:1024 + TT], op=ALU.add))
                xn_chunk_v.append(s_v.n)
            XN_V = s_v.n
            psS_free = (s_v.h, XN_V)
            bank_free["S0"] = (s_v.h, XN_V)
            bank_free["S1"] = (s_v.h, XN_V)

            _chk(tt, "ln")
            (wkvidx, wkvcnt) = wkv_info
            T.wait_ge(s_ws[wkvidx % 3].h, wkvcnt)
            wkv_v = wslot_view(wkvidx)
            nb = 0
            for ko in range(2):
                for (c0, c1) in [(0, 512), (512, TT)]:
                    bank = "A" if (nb % 2 == 0) else "B"
                    nb += 1
                    ps = psA if bank == "A" else psB
                    mm_gate(bank)
                    for ci in range(NCH):
                        T.wait_ge(s_v.h, xn_chunk_v[ci])
                        lhs = wkv_v[:, 512 * ci + 128 * ko: 512 * ci + 128 * ko + 128]
                        last = T.matmul(ps[0:128, 0:c1 - c0], lhs,
                                        xnbuf[:, TT * ci + c0: TT * ci + c1],
                                        start=(ci == 0), stop=(ci == NCH - 1))
                    s_t.inc(last)
                    S.wait_ge(s_t.h, s_t.n)
                    s_s.inc(S.copy(kbuf[:, TT * ko + c0: TT * ko + c1],
                                   ps[0:128, 0:c1 - c0]))
                    bank_free[bank] = (s_s.h, s_s.n)
            for g, (goff, glen) in enumerate(VGROUPS):
                bank = "A" if (g % 2 == 0) else "B"
                ps = psA if bank == "A" else psB
                mm_gate(bank)
                for ci in range(NCH):
                    lhs = xnbuf[:, TT * ci + goff: TT * ci + goff + glen]
                    last = T.matmul(ps[0:glen, 0:256], lhs,
                                    wkv_v[:, 512 * ci + 256: 512 * ci + 512],
                                    start=(ci == 0), stop=(ci == NCH - 1))
                s_t.inc(last)
                V.wait_ge(s_t.h, s_t.n)
                s_v.inc(V.tensor_copy(vbuf[0:glen, 256 * g: 256 * g + 256],
                                      ps[0:glen, 0:256]))
                bank_free[bank] = (s_v.h, s_v.n)
            wslot_free[wkvidx % 3] = (s_t.h, s_t.n)
            wq_info[1] = load_weight_half(wq3, 1)

            _chk(tt, "kv")
            if VEVAC_LAST is not None:
                S.wait_ge(s_v.h, VEVAC_LAST)
            for half in range(2):
                (wqi, wqc) = wq_info[half]
                T.wait_ge(s_ws[wqi % 3].h, wqc)
                wq_v = wslot_view(wqi)
                for co in range(4 * half, 4 * half + 4):
                    bank = ROT4[co % 4]
                    ps = bank_ps(bank)
                    mm_gate(bank)
                    for ci in range(NCH):
                        lhs = wq_v[:, 512 * ci + 128 * (co % 4):
                                   512 * ci + 128 * (co % 4) + 128]
                        last = T.matmul(ps[0:128, 0:512], lhs,
                                        xnbuf[:, TT * ci + HALO: TT * ci + TT],
                                        start=(ci == 0), stop=(ci == NCH - 1))
                    s_t.inc(last)
                    S.wait_ge(s_t.h, s_t.n)
                    s_s.inc(S.copy(qbuf[:, 512 * co: 512 * co + 512],
                                   ps[0:128, 0:512]))
                    bank_free[bank] = (s_s.h, s_s.n)
                wslot_free[wqi % 3] = (s_t.h, s_t.n)
                if half == 0:
                    wo_info = [load_weight_half(wo3, 0)]
                else:
                    wo_info.append(load_weight_half(wo3, 1))
            wr_info = [load_weight_half(wr3, 0), None]
            Q_PROJ_S = s_s.n

            xncur = xnbuf.rearrange("p (c t) -> p c t", t=TT)[:, :, HALO:TT]
            xnprev = xnbuf.rearrange("p (c t) -> p c t", t=TT)[:, :, HALO - 1:TT - 1]
            xm3 = xmix.rearrange("p (c t) -> p c t", t=512)
            if VMM_LAST is not None:
                V.wait_ge(s_t.h, VMM_LAST)
            s_v.inc(V.tensor_tensor(xm3, xncur, xnprev, op=ALU.subtract))
            for ci in range(NCH):
                last = V.scalar_tensor_tensor(
                    xmix[:, 512 * ci: 512 * ci + 512],
                    xmix[:, 512 * ci: 512 * ci + 512], mur[:, ci:ci + 1],
                    xnbuf[:, TT * ci + HALO - 1: TT * ci + TT - 1],
                    op0=ALU.mult, op1=ALU.add)
            s_v.inc(last)
            XR_V = s_v.n
            V.wait_ge(s_t.h, STATS_T)
            sc3 = scr.rearrange("p (c t) -> p c t", t=TT)[:, :, 0:512]
            s_v.inc(V.tensor_tensor(sc3, xncur, xnprev, op=ALU.subtract))
            for ci in range(NCH):
                last = V.scalar_tensor_tensor(
                    scr[:, TT * ci: TT * ci + 512],
                    scr[:, TT * ci: TT * ci + 512], muk[:, ci:ci + 1],
                    xnbuf[:, TT * ci + HALO - 1: TT * ci + TT - 1],
                    op0=ALU.mult, op1=ALU.add)
            s_v.inc(last)
            XK_V = s_v.n

            _chk(tt, "qproj")
            def emit_r_filler(co, bank):
                (wi, wc) = wr_info[0]
                T.wait_ge(s_ws[wi % 3].h, wc)
                T.wait_ge(s_v.h, XR_V)
                wv_ = wslot_view(wi)
                ps = psA if bank == "A" else psB
                mm_gate(bank)
                for ci in range(NCH):
                    lhs = wv_[:, 512 * ci + 128 * (co % 4):
                              512 * ci + 128 * (co % 4) + 128]
                    last = T.matmul(ps[0:128, 0:512], lhs,
                                    xmix[:, 512 * ci: 512 * ci + 512],
                                    start=(ci == 0), stop=(ci == NCH - 1))
                s_t.inc(last)
                S.wait_ge(s_t.h, s_t.n)
                if VEVAC_LAST is not None:
                    S.wait_ge(s_v.h, VEVAC_LAST)
                s_s.inc(S.activation(rbuf2[:, 512 * co: 512 * co + 512],
                                     ps[0:128, 0:512], AF.Sigmoid))
                bank_free[bank] = (s_s.h, s_s.n)

            V.wait_ge(s_s.h, Q_PROJ_S)
            if GADD_LAST is not None:
                V.wait_ge(s_g.h, GADD_LAST)
            mm_gate("A")
            qsq_mm_t = {}
            for ci in range(NCH):
                sl = pbuf[:, 512 * (ci % 2): 512 * (ci % 2) + 512]
                if ci >= 2:
                    V.wait_ge(s_t.h, qsq_mm_t[ci - 2])
                s_v.inc(V.tensor_tensor(sl, qbuf[:, 512 * ci: 512 * ci + 512],
                                        qbuf[:, 512 * ci: 512 * ci + 512],
                                        op=ALU.mult))
                T.wait_ge(s_v.h, s_v.n)
                last = T.matmul(psA[0:NH, 0:512], selq[:, NH * ci: NH * ci + NH],
                                sl, start=(ci == 0), stop=(ci == NCH - 1))
                s_t.inc(last)
                qsq_mm_t[ci] = s_t.n
            emit_r_filler(0, "B")
            V.wait_ge(s_t.h, qsq_mm_t[NCH - 1])
            s_v.inc(V.tensor_scalar_add(qrow[0:NH, 0:512], psA[0:NH, 0:512],
                                        C_EPSRM(NH)))
            bank_free["A"] = (s_v.h, s_v.n)
            s_v.inc(V.reciprocal(qrow[0:NH, 512:1024], qrow[0:NH, 0:512]))
            S.wait_ge(s_v.h, s_v.n)
            s_s.inc(S.activation(qrow[0:NH, 0:512], qrow[0:NH, 512:1024], AF.Sqrt))
            QR_S = s_s.n
            ksl = [pbuf[:, 0:TT], pbuf[:, TT:2 * TT]]
            V.wait_ge(s_t.h, qsq_mm_t[NCH - 1])
            for ci in range(2):
                s_v.inc(V.tensor_tensor(ksl[ci], kbuf[:, TT * ci: TT * ci + TT],
                                        kbuf[:, TT * ci: TT * ci + TT],
                                        op=ALU.mult))
            KSQ_V = s_v.n
            T.wait_ge(s_v.h, KSQ_V)
            T.wait_ge(s_v.h, psPV_rows_done[1])
            for (c0, c1, pcol) in [(0, 512, 0), (512, TT, 512)]:
                for ci in range(2):
                    last = T.matmul(psPV[0:NKV, pcol:pcol + (c1 - c0)],
                                    selk[:, NKV * ci: NKV * ci + NKV],
                                    ksl[ci][:, c0:c1],
                                    start=(ci == 0), stop=(ci == 1))
            s_t.inc(last)
            V.wait_ge(s_t.h, s_t.n)
            V.tensor_scalar_add(krow[0:NKV, 0:512], psPV[0:NKV, 0:512], C_EPSRM(NKV))
            s_v.inc(V.tensor_scalar_add(krow[0:NKV, 512:TT], psPV[0:NKV, 512:TT],
                                        C_EPSRM(NKV)))
            psPV_free = (s_v.h, s_v.n)
            s_v.inc(V.reciprocal(krow[0:NKV, TT:2 * TT], krow[0:NKV, 0:TT]))
            S.wait_ge(s_v.h, s_v.n)
            s_s.inc(S.activation(krow[0:NKV, 0:TT], krow[0:NKV, TT:2 * TT], AF.Sqrt))
            KR_S = s_s.n
            _chk(tt, "rms_rows")
            T.wait_ge(s_s.h, QR_S)
            for ci in range(NCH):
                if ci % 2 == 0:
                    mm_gate("B")
                    ps = psB[0:128, 0:512]
                else:
                    T.wait_ge(psS_free[0], psS_free[1])
                    ps = psS[:, 0:512]
                last = T.matmul(ps, selqT[0:NH, 128 * ci: 128 * ci + 128],
                                qrow[0:NH, 0:512], start=True, stop=True)
                s_t.inc(last)
                V.wait_ge(s_t.h, s_t.n)
                s_v.inc(V.scalar_tensor_tensor(
                    qbuf[:, 512 * ci: 512 * ci + 512],
                    qbuf[:, 512 * ci: 512 * ci + 512], qnw[:, 0:1], ps,
                    op0=ALU.mult, op1=ALU.mult))
                if ci % 2 == 0:
                    bank_free["B"] = (s_v.h, s_v.n)
                else:
                    psS_free = (s_v.h, s_v.n)
            Q_APPLIED_V = s_v.n
            T.wait_ge(s_s.h, KR_S)
            for ci in range(2):
                for (c0, c1) in [(0, 512), (512, TT)]:
                    mm_gate("A")
                    last = T.matmul(psA[0:128, 0:c1 - c0],
                                    selkT[0:NKV, 128 * ci: 128 * ci + 128],
                                    krow[0:NKV, c0:c1], start=True, stop=True)
                    s_t.inc(last)
                    V.wait_ge(s_t.h, s_t.n)
                    s_v.inc(V.scalar_tensor_tensor(
                        kbuf[:, TT * ci + c0: TT * ci + c1],
                        kbuf[:, TT * ci + c0: TT * ci + c1], knw[:, 0:1],
                        psA[0:128, 0:c1 - c0], op0=ALU.mult, op1=ALU.mult))
                    bank_free["A"] = (s_v.h, s_v.n)
            K_APPLIED_V = s_v.n
            emit_r_filler(1, "A")

            _chk(tt, "rms")
            ptbuf_evac_s = None
            pv_evac_v = None
            prev = None

            def emit_qk(ss, qoff, qlen, kw):
                nonlocal_vals = {}
                msk = logmf if (tt == 0 and ss == 0) else (logm if qlen == 112 else logm64)
                T.wait_ge(s_v.h, max(Q_APPLIED_V, K_APPLIED_V))
                if ptbuf_evac_s is not None:
                    T.wait_ge(s_s.h, ptbuf_evac_s)
                else:
                    T.wait_ge(psS_free[0], psS_free[1])
                    mm_gate("S0")
                    mm_gate("S1")
                for j in range(NH):
                    s = (j // 2) + 8 * (j % 2)
                    h = 2 * (s % 8) + s // 8
                    pq = 64 * (h % 2)
                    ciq = h // 2
                    cik = (h % 4) // 2
                    lhs = qbuf[pq:pq + 64, 512 * ciq + qoff: 512 * ciq + qoff + qlen]
                    rhs = kbuf[pq:pq + 64, TT * cik + qoff: TT * cik + qoff + kw]
                    last = T.matmul(psS[0:qlen, 128 * s: 128 * s + kw], lhs, rhs,
                                    start=True, stop=True)
                s_t.inc(last)
                return msk, s_t.n

            def emit_softmax(msk, qk_t, qlen, kw):
                V.wait_ge(s_t.h, qk_t)
                sin = psS[0:qlen, 0:NH * 128].rearrange(
                    "q (h k) -> q h k", h=NH)[:, :, 0:kw]
                min_ = msk[0:qlen, 0:NH * kw].rearrange("q (h k) -> q h k", h=NH)
                pout = pbuf[0:qlen, 0:NH * kw].rearrange("q (h k) -> q h k", h=NH)
                s_v.inc(V.scalar_tensor_tensor(pout, sin, C_SCALE(qlen), min_,
                                               op0=ALU.mult, op1=ALU.add))
                S.wait_ge(s_v.h, s_v.n)
                s_s.inc(S.activation(pbuf[0:qlen, 0:NH * kw],
                                     pbuf[0:qlen, 0:NH * kw], AF.Exp))
                V.wait_ge(s_s.h, s_s.n)
                s_v.inc(V.tensor_reduce(lbuf[0:qlen, 0:NH], pout, axis=AX.X,
                                        op=ALU.add))
                s_v.inc(V.reciprocal(lbuf[0:qlen, NH:2 * NH], lbuf[0:qlen, 0:NH]))
                rb = lbuf[0:qlen, NH:2 * NH].unsqueeze(-1).broadcast_to(
                    [qlen, NH, kw])
                s_v.inc(V.tensor_tensor(pout, pout, rb, op=ALU.mult))
                return s_v.n

            def emit_transposes(pnorm_v, qlen, kw):
                nonlocal ptbuf_evac_s
                T.wait_ge(s_v.h, pnorm_v)
                per_bank = 4 if qlen == 112 else 8
                nbanks = NH // per_bank
                bank_t = []
                for b in range(nbanks):
                    for j in range(per_bank):
                        s = b * per_bank + j
                        tin = pbuf[0:qlen, kw * s: kw * s + kw]
                        col = 512 * b + qlen * j
                        last = T.transpose(psS[0:kw, col: col + qlen], tin,
                                           ident[0:qlen, 0:qlen])
                    s_t.inc(last)
                    bank_t.append(s_t.n)
                for b in range(nbanks):
                    S.wait_ge(s_t.h, bank_t[b])
                    s_s.inc(S.copy(
                        ptbuf[0:kw, per_bank * qlen * b: per_bank * qlen * (b + 1)],
                        psS[0:kw, 512 * b: 512 * b + per_bank * qlen]))
                ptbuf_evac_s = s_s.n

            def emit_pv_oproj(ss, qoff, qlen, kw, ptevac):
                nonlocal pv_evac_v
                T.wait_ge(s_s.h, ptevac)
                if pv_evac_v is not None:
                    T.wait_ge(s_s.h, pv_evac_v)
                else:
                    T.wait_ge(psPV_free[0], psPV_free[1])
                for h in range(NH):
                    s = (h % 2) * 8 + h // 2
                    kv = h % 4
                    lhs = vbuf[0:kw, 256 * ss + 64 * kv: 256 * ss + 64 * kv + 64]
                    rhs = ptbuf[0:kw, qlen * s: qlen * s + qlen]
                    pp = 64 * (h % 2)
                    col = 128 * (h // 2)
                    last = T.matmul(psPV[pp:pp + 64, col: col + qlen], lhs, rhs,
                                    start=True, stop=True)
                s_t.inc(last)
                S.wait_ge(s_t.h, s_t.n)
                pvin = psPV[:, 0:1024].rearrange("p (c t) -> p c t", t=128)[:, :, 0:qlen]
                aout = asub[:, 0:NCH * 112].rearrange(
                    "p (c t) -> p c t", t=112)[:, :, 0:qlen]
                s_s.inc(S.copy(aout, pvin))
                pv_evac_v = s_s.n
                T.wait_ge(s_ws[wo_info[0][0] % 3].h, wo_info[0][1])
                T.wait_ge(s_ws[wo_info[1][0] % 3].h, wo_info[1][1])
                T.wait_ge(s_s.h, pv_evac_v)
                for co in range(NCH):
                    wo_v = wslot_view(wo_info[0][0] if co < 4 else wo_info[1][0])
                    bank = "A" if (co % 2 == 0) else "B"
                    ps = psA if bank == "A" else psB
                    mm_gate(bank)
                    for ci in range(NCH):
                        lhs = wo_v[:, 512 * ci + 128 * (co % 4):
                                   512 * ci + 128 * (co % 4) + 128]
                        last = T.matmul(ps[0:128, 0:qlen], lhs,
                                        asub[:, 112 * ci: 112 * ci + qlen],
                                        start=(ci == 0), stop=(ci == NCH - 1))
                    s_t.inc(last)
                    V.wait_ge(s_t.h, s_t.n)
                    xsl = xbuf[:, TT * co + HALO + qoff: TT * co + HALO + qoff + qlen]
                    ysl = xnbuf[:, TT * co + HALO + qoff: TT * co + HALO + qoff + qlen]
                    s_v.inc(V.scalar_tensor_tensor(ysl, ps[0:128, 0:qlen], C_ONE(128),
                                                   xsl, op0=ALU.mult, op1=ALU.add))
                    bank_free[bank] = (s_v.h, s_v.n)

            for ss, (qoff, qlen) in enumerate(SUBTILES):
                kw = qlen + 16
                msk, qk_t = emit_qk(ss, qoff, qlen, kw)
                pnorm_v = emit_softmax(msk, qk_t, qlen, kw)
                if prev is not None:
                    emit_pv_oproj(*prev)
                emit_transposes(pnorm_v, qlen, kw)
                prev = (ss, qoff, qlen, kw, ptbuf_evac_s)
            emit_pv_oproj(*prev)
            YADDS_V = s_v.n
            PSS_GATE = (s_s.h, ptbuf_evac_s)
            bank_free["S0"] = (s_s.h, ptbuf_evac_s)
            bank_free["S1"] = (s_s.h, ptbuf_evac_s)
            PSPV_GATE = (s_s.h, pv_evac_v)
            wslot_free[wo_info[0][0] % 3] = (s_t.h, s_t.n)
            wslot_free[wo_info[1][0] % 3] = (s_t.h, s_t.n)
            wr_info[1] = load_weight_half(wr3, 1)
            if tt + 1 < NT:
                SY.wait_ge(s_v.h, YADDS_V)
                s_x.inc(SY.dma_start(
                    out=xbuf.rearrange("p (c t) -> p c t", t=TT),
                    in_=xin3[:, :, TILE * (tt + 1): TILE * (tt + 1) + TT]), 16)
                X_READY = s_x.n

            _chk(tt, "attn")

            def ffn_phase(w_infos, rhs_buf, evac, stride=512, co0=0):
                for co in range(co0, NCH):
                    (wi, wc) = w_infos[0] if co < 4 else w_infos[1]
                    T.wait_ge(s_ws[wi % 3].h, wc)
                    wv_ = wslot_view(wi)
                    bank = ROT4[co % 4]
                    ps = bank_ps(bank)
                    mm_gate(bank)
                    for ci in range(NCH):
                        lhs = wv_[:, 512 * ci + 128 * (co % 4):
                                  512 * ci + 128 * (co % 4) + 128]
                        last = T.matmul(ps[0:128, 0:512], lhs,
                                        rhs_buf[:, stride * ci: stride * ci + 512],
                                        start=(ci == 0), stop=(ci == NCH - 1))
                    s_t.inc(last)
                    evac(co, ps, s_t.n, bank)

            _chk(tt, "ffnmix")
            T.wait_ge(s_v.h, XR_V)
            S.wait_ge(s_v.h, XK_V)

            def r_evac(co, ps, tcnt, bank):
                S.wait_ge(s_t.h, tcnt)
                s_s.inc(S.activation(qbuf[:, 512 * co: 512 * co + 512],
                                     ps[0:128, 0:512], AF.Sigmoid))
                bank_free[bank] = (s_s.h, s_s.n)
            ffn_phase(wr_info, xmix, r_evac, co0=2)
            wslot_free[wr_info[0][0] % 3] = (s_t.h, s_t.n)
            wslot_free[wr_info[1][0] % 3] = (s_t.h, s_t.n)
            wkf_info = [load_weight_half(wkf3, 0), load_weight_half(wkf3, 1)]
            _chk(tt, "ffnr")
            T.wait_ge(s_v.h, XK_V)

            def k_evac(co, ps, tcnt, bank):
                V.wait_ge(s_t.h, tcnt)
                s_v.inc(V.tensor_scalar_max(xmix[:, 512 * co: 512 * co + 512],
                                            ps[0:128, 0:512], C_ZERO(128)))
                bank_free[bank] = (s_v.h, s_v.n)
                S.wait_ge(s_v.h, s_v.n)
                s_s.inc(S.activation(xmix[:, 512 * co: 512 * co + 512],
                                     xmix[:, 512 * co: 512 * co + 512], AF.Square))
            ffn_phase(wkf_info, scr, k_evac, stride=TT)
            KMM_LAST = s_t.n
            wslot_free[wkf_info[0][0] % 3] = (s_t.h, s_t.n)
            wslot_free[wkf_info[1][0] % 3] = (s_t.h, s_t.n)
            wvf_info = [load_weight_half(wvf3, 0), load_weight_half(wvf3, 1)]
            _chk(tt, "ffnk")
            KACT_S = s_s.n
            T.wait_ge(s_s.h, KACT_S)

            gadd_n = {}

            def v_evac(co, ps, tcnt, bank):
                V.wait_ge(s_t.h, tcnt)
                tsl = pbuf[:, 512 * (co % 2): 512 * (co % 2) + 512]
                if co >= 2:
                    V.wait_ge(s_g.h, gadd_n[co - 2])
                rsrc = (rbuf2[:, 512 * co: 512 * co + 512] if co < 2
                        else qbuf[:, 512 * co: 512 * co + 512])
                s_v.inc(V.tensor_tensor(tsl, ps[0:128, 0:512], rsrc,
                                        op=ALU.mult))
                bank_free[bank] = (s_v.h, s_v.n)
                G.wait_ge(s_v.h, s_v.n)
                ysl = xnbuf[:, TT * co + HALO: TT * co + TT]
                s_g.inc(G.tensor_tensor(ysl, ysl, tsl, op=ALU.add))
                gadd_n[co] = s_g.n
            ffn_phase(wvf_info, xmix, v_evac)
            VMM_LAST = s_t.n
            VEVAC_LAST = s_v.n
            GADD_LAST = s_g.n
            wslot_free[wvf_info[0][0] % 3] = (s_t.h, s_t.n)
            wslot_free[wvf_info[1][0] % 3] = (s_t.h, s_t.n)
            if tt + 1 < NT:
                wkv_info = load_weight_half(None, 0, kvpair=True)
                wq_info = [load_weight_half(wq3, 0), None]

            SY.wait_ge(s_g.h, s_g.n)
            s_y.inc(SY.dma_start(
                out=yout3[:, :, TILE * tt: TILE * tt + TILE],
                in_=xnbuf.rearrange("p (c t) -> p c t", t=TT)[:, :, HALO:TT]), 16)
        except _StopBuild:
            SY.wait_ge(s_t.h, s_t.n)
            SY.wait_ge(s_v.h, s_v.n)
            SY.wait_ge(s_s.h, s_s.n)
            SY.wait_ge(s_g.h, s_g.n)
            done = s_y.n // 16
            for t2 in range(done, NT):
                s_y.inc(SY.dma_start(
                    out=yout3[:, :, TILE * t2: TILE * t2 + TILE],
                    in_=xbuf.rearrange("p (c t) -> p c t", t=TT)[:, :, HALO:TT]), 16)

        G.wait_ge(s_y.h, 16 * NT)

        block.sync(SY._flush)
        block.tensor(T._flush)
        block.vector(V._flush)
        block.scalar(S._flush)
        block.gpsimd(G._flush)

    return nc


_NC_CACHE = [None]


def kernel(L0_attn_x, L0_attn_wq, L0_attn_wk, L0_attn_wv, L0_attn_wo,
           L0_attn_q_norm_w, L0_attn_k_norm_w, L0_ffn_W_r, L0_ffn_W_k,
           L0_ffn_W_v, L0_ffn_mu_r, L0_ffn_mu_k):
    x = np.asarray(L0_attn_x, dtype=np.float32)
    f32c = lambda a: np.ascontiguousarray(np.asarray(a, dtype=np.float32))
    wq_t = f32c(np.asarray(L0_attn_wq).T)
    wk_t = f32c(np.asarray(L0_attn_wk).T)
    wv_t = f32c(np.asarray(L0_attn_wv).T)
    wo_t = f32c(np.asarray(L0_attn_wo).T)
    wr_t = f32c(np.asarray(L0_ffn_W_r).T)
    wkf_t = f32c(np.asarray(L0_ffn_W_k).T)
    wvf_t = f32c(np.asarray(L0_ffn_W_v).T)
    qnw = f32c(np.tile(np.asarray(L0_attn_q_norm_w), 2)[:, None])
    knw = f32c(np.tile(np.asarray(L0_attn_k_norm_w), 2)[:, None])
    mur = f32c(np.asarray(L0_ffn_mu_r).reshape(NCH, 128).T)
    muk = f32c(np.asarray(L0_ffn_mu_k).reshape(NCH, 128).T)
    scal = f32c(np.tile(np.array([0.125, 1e-5, 0.01, -1.0, 1.0, 0.0, 0.0, 0.0],
                                 dtype=np.float32), (128, 1)))
    cn = _consts()

    in_maps = []
    for c in range(8):
        b, h = c // 2, c % 2
        xt = x[b].T
        if h == 0:
            xcore = np.concatenate(
                [np.zeros((D, HALO), dtype=np.float32), xt[:, :CORE_T]], axis=1)
            lmf = cn["logmf"]
        else:
            xcore = xt[:, CORE_T - HALO: 2 * CORE_T]
            lmf = cn["logm"]
        in_maps.append({
            "xin": np.ascontiguousarray(xcore),
            "w_wq": wq_t, "w_wk": wk_t, "w_wv": wv_t, "w_wo": wo_t,
            "w_wr": wr_t, "w_wkf": wkf_t, "w_wvf": wvf_t,
            "c_ident": cn["ident"], "c_logm": cn["logm"], "c_logmf": lmf,
            "c_logm64": cn["logm64"], "c_selq": cn["selq"], "c_selqT": cn["selqT"],
            "c_selk": cn["selk"], "c_selkT": cn["selkT"],
            "c_qnw": qnw, "c_knw": knw, "c_mur": mur, "c_muk": muk, "c_scal": scal,
        })

    if _NC_CACHE[0] is None:
        _NC_CACHE[0] = build_program()
    nc = _NC_CACHE[0]
    res = run_bass_kernel_spmd(nc, in_maps, core_ids=list(range(8)))

    y = np.empty((B, L, D), dtype=np.float32)
    for c in range(8):
        b, h = c // 2, c % 2
        y[b, h * CORE_T:(h + 1) * CORE_T, :] = res.results[c]["yout"].T
    return y


# version 27
# speedup vs baseline: 1.0073x; 1.0073x over previous
import math
import numpy as np
import concourse.bass as bass
import concourse.mybir as mybir
from concourse.bass_utils import run_bass_kernel_spmd

F32 = mybir.dt.float32
AF = mybir.ActivationFunctionType
ALU = mybir.AluOpType
AX = mybir.AxisListType

D = 1024
NCH = 8
NH = 16
NKV = 4
HD = 64
B, L = 4, 4096
CORE_T = 2048
HALO = 16
TILE = 512
NT = CORE_T // TILE
TT = TILE + HALO
SUBTILES = [(0, 112), (112, 112), (224, 112), (336, 112), (448, 64)]
VGROUPS = [(0, 128), (112, 128), (224, 128), (336, 128), (448, 80)]
NEG = -1e30
SHIFT = 8.0


def _slopes():
    return np.array([2.0 ** (-0.5 * (h + 1)) for h in range(NH)], dtype=np.float64)


def _mask_tile(qlen, first):
    kw = qlen + 16
    sl = _slopes()
    m = np.full((qlen, NH, kw), NEG, dtype=np.float64)
    for i in range(qlen):
        for j in range(i, min(kw, i + 17)):
            if first and j < 16:
                continue
            m[i, :, j] = sl * (j - i - 16) - SHIFT
    m = m[:, [2 * (s % 8) + s // 8 for s in range(NH)], :]
    return m.astype(np.float32).reshape(qlen, NH * kw)


def _consts():
    c = {}
    c["ident"] = np.eye(128, dtype=np.float32)
    c["logm"] = _mask_tile(112, False)
    c["logmf"] = _mask_tile(112, True)
    c["logm64"] = _mask_tile(64, False)
    selq = np.zeros((128, NCH * NH), dtype=np.float32)
    selqT = np.zeros((NH, NCH * 128), dtype=np.float32)
    for ci in range(NCH):
        for p in range(128):
            h = 2 * ci + (1 if p >= 64 else 0)
            selq[p, 16 * ci + h] = 1.0 / 64.0
            selqT[h, 128 * ci + p] = 1.0
    c["selq"], c["selqT"] = selq, selqT
    selk = np.zeros((128, 2 * NKV), dtype=np.float32)
    selkT = np.zeros((NKV, 2 * 128), dtype=np.float32)
    for ci in range(2):
        for p in range(128):
            kv = 2 * ci + (1 if p >= 64 else 0)
            selk[p, 4 * ci + kv] = 1.0 / 64.0
            selkT[kv, 128 * ci + p] = 1.0
    c["selk"], c["selkT"] = selk, selkT
    return c


_DBG_STOP = [None]


class _StopBuild(Exception):
    pass


def _chk(tt, tag):
    st = _DBG_STOP[0]
    if st is None:
        return
    stile, stag = st
    if tt > stile or (tt == stile and tag == stag):
        raise _StopBuild()


class _Sem:
    def __init__(self, nc, name):
        self.h = nc.semaphore(name).__enter__()
        self.n = 0

    def inc(self, inst, k=1):
        inst.then_inc(self.h, k)
        self.n += k


class _DefInst:
    __slots__ = ("incs",)

    def __init__(self):
        self.incs = []

    def then_inc(self, sem, k):
        self.incs.append((sem, k))
        return self


def _ap_span(ap):
    try:
        t = ap.tensor.name
        pairs = [list(p) for p in ap.ap]
        bp = ap.base_partition
        if callable(bp):
            bp = bp()
        bp = bp or 0
        np_ = pairs[0][1] if pairs else 128
        lo = ap.offset
        hi = ap.offset
        for st, cnt in pairs[1:]:
            if st > 0:
                hi += st * (cnt - 1)
            elif st < 0:
                lo += st * (cnt - 1)
        return (t, bp, bp + np_, lo, hi + 1)
    except Exception:
        return None


def _overlap(s1, s2):
    return (s1[0] == s2[0] and s1[1] < s2[2] and s2[1] < s1[2]
            and s1[3] < s2[4] and s2[3] < s1[4])


class _EngRec:

    _HAZARD = ("vector", "scalar", "gpsimd")

    def __init__(self, name):
        self._name = name
        self._ops = []
        self._wr = []
        self._rd = []

    def __getattr__(self, attr):
        if attr.startswith("_"):
            raise AttributeError(attr)
        rec = self

        def call(*a, **kw):
            if rec._name in _EngRec._HAZARD and attr not in ("wait_ge", "drain"):
                wsp, rsp = [], []
                args = list(a)
                for i, arg in enumerate(args):
                    sp = _ap_span(arg) if hasattr(arg, "tensor") else None
                    if sp is None:
                        continue
                    (wsp if i == 0 else rsp).append(sp)
                for k2, arg in kw.items():
                    sp = _ap_span(arg) if hasattr(arg, "tensor") else None
                    if sp is None:
                        continue
                    (wsp if k2 == "accum_out" else rsp).append(sp)
                conflict = any(_overlap(s, w) for s in wsp + rsp for w in rec._wr)                     or any(_overlap(s, r) for s in wsp for r in rec._rd)
                if conflict:
                    rec._ops.append(("drain", (), {}, _DefInst()))
                    rec._wr, rec._rd = [], []
                rec._wr.extend(wsp)
                rec._rd.extend(rsp)
            di = _DefInst()
            rec._ops.append((attr, a, kw, di))
            return di
        return call

    def _flush(self, eng):
        for attr, a, kw, di in self._ops:
            inst = getattr(eng, attr)(*a, **kw)
            for sem, k in di.incs:
                inst.then_inc(sem, k)


def build_program():
    nc = bass.Bass("TRN2", target_bir_lowering=False)
    xin = nc.dram_tensor("xin", [D, CORE_T + HALO], F32, kind="ExternalInput").ap()
    yout = nc.dram_tensor("yout", [D, CORE_T], F32, kind="ExternalOutput").ap()
    w_wq = nc.dram_tensor("w_wq", [D, D], F32, kind="ExternalInput").ap()
    w_wk = nc.dram_tensor("w_wk", [D, 256], F32, kind="ExternalInput").ap()
    w_wv = nc.dram_tensor("w_wv", [D, 256], F32, kind="ExternalInput").ap()
    w_wo = nc.dram_tensor("w_wo", [D, D], F32, kind="ExternalInput").ap()
    w_wr = nc.dram_tensor("w_wr", [D, D], F32, kind="ExternalInput").ap()
    w_wkf = nc.dram_tensor("w_wkf", [D, D], F32, kind="ExternalInput").ap()
    w_wvf = nc.dram_tensor("w_wvf", [D, D], F32, kind="ExternalInput").ap()
    c_ident = nc.dram_tensor("c_ident", [128, 128], F32, kind="ExternalInput").ap()
    c_logm = nc.dram_tensor("c_logm", [112, NH * 128], F32, kind="ExternalInput").ap()
    c_logmf = nc.dram_tensor("c_logmf", [112, NH * 128], F32, kind="ExternalInput").ap()
    c_logm64 = nc.dram_tensor("c_logm64", [64, NH * 80], F32, kind="ExternalInput").ap()
    c_selq = nc.dram_tensor("c_selq", [128, NCH * NH], F32, kind="ExternalInput").ap()
    c_selqT = nc.dram_tensor("c_selqT", [NH, NCH * 128], F32, kind="ExternalInput").ap()
    c_selk = nc.dram_tensor("c_selk", [128, 2 * NKV], F32, kind="ExternalInput").ap()
    c_selkT = nc.dram_tensor("c_selkT", [NKV, 2 * 128], F32, kind="ExternalInput").ap()
    c_qnw = nc.dram_tensor("c_qnw", [128, 1], F32, kind="ExternalInput").ap()
    c_knw = nc.dram_tensor("c_knw", [128, 1], F32, kind="ExternalInput").ap()
    c_mur = nc.dram_tensor("c_mur", [128, NCH], F32, kind="ExternalInput").ap()
    c_muk = nc.dram_tensor("c_muk", [128, NCH], F32, kind="ExternalInput").ap()
    c_scal = nc.dram_tensor("c_scal", [128, 8], F32, kind="ExternalInput").ap()

    sb = lambda name, shape: nc.alloc_sbuf_tensor(name, shape, F32).ap()
    pm = lambda name, shape: nc.alloc_psum_tensor(name, shape, F32).ap()
    xbuf = sb("xbuf", [128, NCH * TT])
    xnbuf = sb("xnbuf", [128, NCH * TT])
    scr = sb("scr", [128, NCH * TT])
    qbuf = sb("qbuf", [128, NCH * TILE])
    xmix = sb("xmix", [128, NCH * TILE])
    kbuf = sb("kbuf", [128, 2 * TT])
    vbuf = sb("vbuf", [128, 5 * 256])
    pbuf = sb("pbuf", [128, NH * 128])
    ptbuf = sb("ptbuf", [128, NH * 112])
    asub = sb("asub", [128, NCH * 112])
    lbuf = sb("lbuf", [128, 2 * NH])
    rbuf2 = sb("rbuf2", [128, 2 * 512])
    qrow = sb("qrow", [NH, 2 * TILE])
    krow = sb("krow", [NKV, 2 * TT])
    lrow = sb("lrow", [1, 4 * TT])
    wslot = sb("wslot", [128, 3 * 4096])
    ident = sb("ident", [128, 128])
    logm = sb("logm", [112, NH * 128])
    logmf = sb("logmf", [112, NH * 128])
    logm64 = sb("logm64", [64, NH * 80])
    selq = sb("selq", [128, NCH * NH])
    selqT = sb("selqT", [NH, NCH * 128])
    selk = sb("selk", [128, 2 * NKV])
    selkT = sb("selkT", [NKV, 2 * 128])
    qnw = sb("qnw", [128, 1])
    knw = sb("knw", [128, 1])
    mur = sb("mur", [128, NCH])
    muk = sb("muk", [128, NCH])
    scal = sb("scal", [128, 8])
    ones1 = sb("ones1", [1, 128])
    invD = sb("invD", [128, 1])
    psS = pm("psS", [128, 2048])
    psPV = pm("psPV", [128, 1024])
    psA = pm("psA", [128, 512])
    psB = pm("psB", [128, 512])
    with nc.Block() as block:
        s_c = _Sem(nc, "s_c")
        s_ws = [_Sem(nc, "s_ws0"), _Sem(nc, "s_ws1"), _Sem(nc, "s_ws2")]
        s_x = _Sem(nc, "s_x")
        s_y = _Sem(nc, "s_y")
        s_t = _Sem(nc, "s_t")
        s_v = _Sem(nc, "s_v")
        s_s = _Sem(nc, "s_s")
        s_g = _Sem(nc, "s_g")

        T = _EngRec("tensor")
        V = _EngRec("vector")
        S = _EngRec("scalar")
        G = _EngRec("gpsimd")
        SY = _EngRec("sync")

        C_SCALE = lambda p: scal[0:p, 0:1]
        C_EPSLN = lambda p: scal[0:p, 1:2]
        C_EPSRM = lambda p: scal[0:p, 2:3]
        C_NEG1 = lambda p: scal[0:p, 3:4]
        C_ONE = lambda p: scal[0:p, 4:5]
        C_ZERO = lambda p: scal[0:p, 5:6]

        for dst, src in [(scal, c_scal), (ident, c_ident), (logm, c_logm),
                         (logmf, c_logmf), (logm64, c_logm64), (selq, c_selq),
                         (selqT, c_selqT), (selk, c_selk), (selkT, c_selkT),
                         (qnw, c_qnw), (knw, c_knw), (mur, c_mur), (muk, c_muk)]:
            s_c.inc(SY.dma_start(out=dst[:, :], in_=src[:, :]), 16)
        CONST_W = s_c.n
        V.wait_ge(s_c.h, CONST_W)
        V.memset(ones1[:, :], 1.0)
        s_v.inc(V.memset(invD[:, :], 1.0 / float(D)))
        INIT_V = s_v.n

        wq3 = w_wq.rearrange("(c p) o -> p c o", p=128)
        wo3 = w_wo.rearrange("(c p) o -> p c o", p=128)
        wr3 = w_wr.rearrange("(c p) o -> p c o", p=128)
        wkf3 = w_wkf.rearrange("(c p) o -> p c o", p=128)
        wvf3 = w_wvf.rearrange("(c p) o -> p c o", p=128)
        wk3 = w_wk.rearrange("(c p) o -> p c o", p=128)
        wv3 = w_wv.rearrange("(c p) o -> p c o", p=128)
        xin3 = xin.rearrange("(c p) t -> p c t", p=128)
        yout3 = yout.rearrange("(c p) t -> p c t", p=128)

        wload_idx = [0]
        wslot_free = [None, None, None]

        def wslot_view(idx):
            sl = idx % 3
            return wslot[:, 4096 * sl: 4096 * (sl + 1)]

        def load_weight_half(src3, half, kvpair=False):
            idx = wload_idx[0]
            sl = idx % 3
            if wslot_free[sl] is not None:
                sem, cnt = wslot_free[sl]
                SY.wait_ge(sem, cnt)
            dst = wslot_view(idx)
            if kvpair:
                s_ws[sl].inc(SY.dma_start(
                    out=dst.rearrange("p (c o) -> p c o", o=512)[:, :, 0:256],
                    in_=wk3[:, :, :]), 16)
                s_ws[sl].inc(SY.dma_start(
                    out=dst.rearrange("p (c o) -> p c o", o=512)[:, :, 256:512],
                    in_=wv3[:, :, :]), 16)
            else:
                s_ws[sl].inc(SY.dma_start(
                    out=dst.rearrange("p (c o) -> p c o", o=512),
                    in_=src3[:, :, 512 * half: 512 * (half + 1)]), 16)
            wload_idx[0] += 1
            return (idx, s_ws[sl].n)

        wkv_info = load_weight_half(None, 0, kvpair=True)
        wq_info = [load_weight_half(wq3, 0), None]

        bank_free = {"A": None, "B": None, "S0": None, "S1": None}
        ROT4 = ["A", "B", "S0", "S1"]

        def bank_ps(bank):
            if bank == "A":
                return psA
            if bank == "B":
                return psB
            if bank == "S0":
                return psS[:, 0:512]
            return psS[:, 512:1024]

        def mm_gate(bank):
            if bank_free[bank] is not None:
                sem, cnt = bank_free[bank]
                T.wait_ge(sem, cnt)

        X_READY = None
        KMM_LAST = None
        VMM_LAST = None
        GADD_LAST = None
        PSS_GATE = None
        PSPV_GATE = None
        VEVAC_LAST = None
        try:
          for tt in range(NT):
            if tt == 0:
                s_x.inc(SY.dma_start(
                    out=xbuf.rearrange("p (c t) -> p c t", t=TT),
                    in_=xin3[:, :, 0:TT]), 16)
                X_READY = s_x.n

            V.wait_ge(s_x.h, X_READY)
            if tt == 0:
                V.wait_ge(s_c.h, CONST_W)
            if KMM_LAST is not None:
                V.wait_ge(s_t.h, KMM_LAST)
            s_v.inc(V.tensor_tensor(scr[:, 0:NCH * TT], xbuf[:, 0:NCH * TT],
                                    xbuf[:, 0:NCH * TT], op=ALU.mult))
            XSQ_V = s_v.n
            T.wait_ge(s_x.h, X_READY)
            T.wait_ge(s_v.h, INIT_V)
            if PSPV_GATE is not None:
                T.wait_ge(PSPV_GATE[0], PSPV_GATE[1])
            mm_gate("A")
            for ci in range(NCH):
                T.matmul(psA[0:1, 0:512], invD[:, 0:1],
                         xbuf[:, TT * ci: TT * ci + 512],
                         start=(ci == 0), stop=(ci == NCH - 1))
            for ci in range(NCH):
                T.matmul(psPV[0:1, 0:16], invD[:, 0:1],
                         xbuf[:, TT * ci + 512: TT * ci + TT],
                         start=(ci == 0), stop=(ci == NCH - 1))
            T.wait_ge(s_v.h, XSQ_V)
            mm_gate("B")
            for ci in range(NCH):
                T.matmul(psB[0:1, 0:512], invD[:, 0:1],
                         scr[:, TT * ci: TT * ci + 512],
                         start=(ci == 0), stop=(ci == NCH - 1))
            for ci in range(NCH):
                inst = T.matmul(psPV[0:1, 512:528], invD[:, 0:1],
                                scr[:, TT * ci + 512: TT * ci + TT],
                                start=(ci == 0), stop=(ci == NCH - 1))
            s_t.inc(inst)
            STATS_T = s_t.n
            r0 = lrow[0:1, 0:TT]
            r1 = lrow[0:1, TT:2 * TT]
            r2 = lrow[0:1, 2 * TT:3 * TT]
            r3 = lrow[0:1, 3 * TT:4 * TT]
            V.wait_ge(s_t.h, STATS_T)
            V.tensor_copy(lrow[0:1, 0:512], psA[0:1, 0:512])
            V.tensor_copy(lrow[0:1, 512:TT], psPV[0:1, 0:16])
            V.tensor_copy(lrow[0:1, TT:TT + 512], psB[0:1, 0:512])
            s_v.inc(V.tensor_copy(lrow[0:1, TT + 512:2 * TT], psPV[0:1, 512:528]))
            bank_free["A"] = (s_v.h, s_v.n)
            bank_free["B"] = (s_v.h, s_v.n)
            psPV_rows_done = (s_v.h, s_v.n)
            V.tensor_tensor(r2, r0, r0, op=ALU.mult)
            V.tensor_tensor(r1, r1, r2, op=ALU.subtract)
            V.tensor_scalar_add(r1, r1, C_EPSLN(1))
            s_v.inc(V.reciprocal(r2, r1))
            S.wait_ge(s_v.h, s_v.n)
            s_s.inc(S.activation(r3, r2, AF.Sqrt))
            V.wait_ge(s_s.h, s_s.n)
            s_v.inc(V.scalar_tensor_tensor(r2, r0, C_NEG1(1), r3,
                                           op0=ALU.mult, op1=ALU.mult))
            LNROWS_V = s_v.n
            T.wait_ge(s_v.h, LNROWS_V)
            if PSS_GATE is not None:
                T.wait_ge(PSS_GATE[0], PSS_GATE[1])
            mm_gate("S0")
            mm_gate("S1")
            T.matmul(psS[:, 0:512], ones1[0:1, :], r3[0:1, 0:512], start=True, stop=True)
            T.matmul(psS[:, 512:TT], ones1[0:1, :], r3[0:1, 512:TT], start=True, stop=True)
            T.matmul(psS[:, 1024:1536], ones1[0:1, :], r2[0:1, 0:512], start=True, stop=True)
            inst = T.matmul(psS[:, 1536:1024 + TT], ones1[0:1, :], r2[0:1, 512:TT],
                            start=True, stop=True)
            s_t.inc(inst)
            LNB_T = s_t.n
            V.wait_ge(s_t.h, LNB_T)
            if tt > 0:
                V.wait_ge(s_y.h, 16 * tt)
            xn_chunk_v = []
            for ci in range(NCH):
                xs = xbuf[:, TT * ci: TT * ci + TT]
                xn_ = xnbuf[:, TT * ci: TT * ci + TT]
                V.tensor_tensor(xn_, xs, psS[:, 0:TT], op=ALU.mult)
                s_v.inc(V.tensor_tensor(xn_, xn_, psS[:, 1024:1024 + TT], op=ALU.add))
                xn_chunk_v.append(s_v.n)
            XN_V = s_v.n
            psS_free = (s_v.h, XN_V)
            bank_free["S0"] = (s_v.h, XN_V)
            bank_free["S1"] = (s_v.h, XN_V)

            _chk(tt, "ln")
            (wkvidx, wkvcnt) = wkv_info
            T.wait_ge(s_ws[wkvidx % 3].h, wkvcnt)
            wkv_v = wslot_view(wkvidx)
            nb = 0
            for ko in range(2):
                for (c0, c1) in [(0, 512), (512, TT)]:
                    bank = "A" if (nb % 2 == 0) else "B"
                    nb += 1
                    ps = psA if bank == "A" else psB
                    mm_gate(bank)
                    for ci in range(NCH):
                        T.wait_ge(s_v.h, xn_chunk_v[ci])
                        lhs = wkv_v[:, 512 * ci + 128 * ko: 512 * ci + 128 * ko + 128]
                        last = T.matmul(ps[0:128, 0:c1 - c0], lhs,
                                        xnbuf[:, TT * ci + c0: TT * ci + c1],
                                        start=(ci == 0), stop=(ci == NCH - 1))
                    s_t.inc(last)
                    S.wait_ge(s_t.h, s_t.n)
                    s_s.inc(S.copy(kbuf[:, TT * ko + c0: TT * ko + c1],
                                   ps[0:128, 0:c1 - c0]))
                    bank_free[bank] = (s_s.h, s_s.n)
            for g, (goff, glen) in enumerate(VGROUPS):
                bank = "A" if (g % 2 == 0) else "B"
                ps = psA if bank == "A" else psB
                mm_gate(bank)
                for ci in range(NCH):
                    lhs = xnbuf[:, TT * ci + goff: TT * ci + goff + glen]
                    last = T.matmul(ps[0:glen, 0:256], lhs,
                                    wkv_v[:, 512 * ci + 256: 512 * ci + 512],
                                    start=(ci == 0), stop=(ci == NCH - 1))
                s_t.inc(last)
                V.wait_ge(s_t.h, s_t.n)
                s_v.inc(V.tensor_copy(vbuf[0:glen, 256 * g: 256 * g + 256],
                                      ps[0:glen, 0:256]))
                bank_free[bank] = (s_v.h, s_v.n)
            wslot_free[wkvidx % 3] = (s_t.h, s_t.n)
            wq_info[1] = load_weight_half(wq3, 1)

            _chk(tt, "kv")
            if VEVAC_LAST is not None:
                S.wait_ge(s_v.h, VEVAC_LAST)
            for half in range(2):
                (wqi, wqc) = wq_info[half]
                T.wait_ge(s_ws[wqi % 3].h, wqc)
                wq_v = wslot_view(wqi)
                for co in range(4 * half, 4 * half + 4):
                    bank = ROT4[co % 4]
                    ps = bank_ps(bank)
                    mm_gate(bank)
                    for ci in range(NCH):
                        lhs = wq_v[:, 512 * ci + 128 * (co % 4):
                                   512 * ci + 128 * (co % 4) + 128]
                        last = T.matmul(ps[0:128, 0:512], lhs,
                                        xnbuf[:, TT * ci + HALO: TT * ci + TT],
                                        start=(ci == 0), stop=(ci == NCH - 1))
                    s_t.inc(last)
                    S.wait_ge(s_t.h, s_t.n)
                    s_s.inc(S.copy(qbuf[:, 512 * co: 512 * co + 512],
                                   ps[0:128, 0:512]))
                    bank_free[bank] = (s_s.h, s_s.n)
                wslot_free[wqi % 3] = (s_t.h, s_t.n)
                if half == 0:
                    wo_info = [load_weight_half(wo3, 0)]
                else:
                    wo_info.append(load_weight_half(wo3, 1))
            wr_info = [load_weight_half(wr3, 0), None]
            Q_PROJ_S = s_s.n

            xncur = xnbuf.rearrange("p (c t) -> p c t", t=TT)[:, :, HALO:TT]
            xnprev = xnbuf.rearrange("p (c t) -> p c t", t=TT)[:, :, HALO - 1:TT - 1]
            xm3 = xmix.rearrange("p (c t) -> p c t", t=512)
            if VMM_LAST is not None:
                V.wait_ge(s_t.h, VMM_LAST)
            s_v.inc(V.tensor_tensor(xm3, xncur, xnprev, op=ALU.subtract))
            for ci in range(NCH):
                last = V.scalar_tensor_tensor(
                    xmix[:, 512 * ci: 512 * ci + 512],
                    xmix[:, 512 * ci: 512 * ci + 512], mur[:, ci:ci + 1],
                    xnbuf[:, TT * ci + HALO - 1: TT * ci + TT - 1],
                    op0=ALU.mult, op1=ALU.add)
            s_v.inc(last)
            XR_V = s_v.n
            V.wait_ge(s_t.h, STATS_T)
            sc3 = scr.rearrange("p (c t) -> p c t", t=TT)[:, :, 0:512]
            s_v.inc(V.tensor_tensor(sc3, xncur, xnprev, op=ALU.subtract))
            for ci in range(NCH):
                last = V.scalar_tensor_tensor(
                    scr[:, TT * ci: TT * ci + 512],
                    scr[:, TT * ci: TT * ci + 512], muk[:, ci:ci + 1],
                    xnbuf[:, TT * ci + HALO - 1: TT * ci + TT - 1],
                    op0=ALU.mult, op1=ALU.add)
            s_v.inc(last)
            XK_V = s_v.n

            _chk(tt, "qproj")
            def emit_r_filler(co, bank):
                (wi, wc) = wr_info[0]
                T.wait_ge(s_ws[wi % 3].h, wc)
                T.wait_ge(s_v.h, XR_V)
                wv_ = wslot_view(wi)
                ps = psA if bank == "A" else psB
                mm_gate(bank)
                for ci in range(NCH):
                    lhs = wv_[:, 512 * ci + 128 * (co % 4):
                              512 * ci + 128 * (co % 4) + 128]
                    last = T.matmul(ps[0:128, 0:512], lhs,
                                    xmix[:, 512 * ci: 512 * ci + 512],
                                    start=(ci == 0), stop=(ci == NCH - 1))
                s_t.inc(last)
                S.wait_ge(s_t.h, s_t.n)
                if VEVAC_LAST is not None:
                    S.wait_ge(s_v.h, VEVAC_LAST)
                s_s.inc(S.activation(rbuf2[:, 512 * co: 512 * co + 512],
                                     ps[0:128, 0:512], AF.Sigmoid))
                bank_free[bank] = (s_s.h, s_s.n)

            V.wait_ge(s_s.h, Q_PROJ_S)
            if GADD_LAST is not None:
                V.wait_ge(s_g.h, GADD_LAST)
            mm_gate("A")
            qsq_mm_t = {}
            for ci in range(NCH):
                sl = pbuf[:, 512 * (ci % 2): 512 * (ci % 2) + 512]
                if ci >= 2:
                    V.wait_ge(s_t.h, qsq_mm_t[ci - 2])
                s_v.inc(V.tensor_tensor(sl, qbuf[:, 512 * ci: 512 * ci + 512],
                                        qbuf[:, 512 * ci: 512 * ci + 512],
                                        op=ALU.mult))
                T.wait_ge(s_v.h, s_v.n)
                last = T.matmul(psA[0:NH, 0:512], selq[:, NH * ci: NH * ci + NH],
                                sl, start=(ci == 0), stop=(ci == NCH - 1))
                s_t.inc(last)
                qsq_mm_t[ci] = s_t.n
            emit_r_filler(0, "B")
            V.wait_ge(s_t.h, qsq_mm_t[NCH - 1])
            s_v.inc(V.tensor_scalar_add(qrow[0:NH, 0:512], psA[0:NH, 0:512],
                                        C_EPSRM(NH)))
            bank_free["A"] = (s_v.h, s_v.n)
            s_v.inc(V.reciprocal(qrow[0:NH, 512:1024], qrow[0:NH, 0:512]))
            S.wait_ge(s_v.h, s_v.n)
            s_s.inc(S.activation(qrow[0:NH, 0:512], qrow[0:NH, 512:1024], AF.Sqrt))
            QR_S = s_s.n
            ksl = [pbuf[:, 0:TT], pbuf[:, TT:2 * TT]]
            V.wait_ge(s_t.h, qsq_mm_t[NCH - 1])
            for ci in range(2):
                s_v.inc(V.tensor_tensor(ksl[ci], kbuf[:, TT * ci: TT * ci + TT],
                                        kbuf[:, TT * ci: TT * ci + TT],
                                        op=ALU.mult))
            KSQ_V = s_v.n
            T.wait_ge(s_v.h, KSQ_V)
            T.wait_ge(s_v.h, psPV_rows_done[1])
            for (c0, c1, pcol) in [(0, 512, 0), (512, TT, 512)]:
                for ci in range(2):
                    last = T.matmul(psPV[0:NKV, pcol:pcol + (c1 - c0)],
                                    selk[:, NKV * ci: NKV * ci + NKV],
                                    ksl[ci][:, c0:c1],
                                    start=(ci == 0), stop=(ci == 1))
            s_t.inc(last)
            V.wait_ge(s_t.h, s_t.n)
            V.tensor_scalar_add(krow[0:NKV, 0:512], psPV[0:NKV, 0:512], C_EPSRM(NKV))
            s_v.inc(V.tensor_scalar_add(krow[0:NKV, 512:TT], psPV[0:NKV, 512:TT],
                                        C_EPSRM(NKV)))
            psPV_free = (s_v.h, s_v.n)
            s_v.inc(V.reciprocal(krow[0:NKV, TT:2 * TT], krow[0:NKV, 0:TT]))
            S.wait_ge(s_v.h, s_v.n)
            s_s.inc(S.activation(krow[0:NKV, 0:TT], krow[0:NKV, TT:2 * TT], AF.Sqrt))
            KR_S = s_s.n
            _chk(tt, "rms_rows")
            T.wait_ge(s_s.h, QR_S)
            for ci in range(NCH):
                if ci % 2 == 0:
                    mm_gate("B")
                    ps = psB[0:128, 0:512]
                else:
                    T.wait_ge(psS_free[0], psS_free[1])
                    ps = psS[:, 0:512]
                last = T.matmul(ps, selqT[0:NH, 128 * ci: 128 * ci + 128],
                                qrow[0:NH, 0:512], start=True, stop=True)
                s_t.inc(last)
                V.wait_ge(s_t.h, s_t.n)
                s_v.inc(V.scalar_tensor_tensor(
                    qbuf[:, 512 * ci: 512 * ci + 512],
                    qbuf[:, 512 * ci: 512 * ci + 512], qnw[:, 0:1], ps,
                    op0=ALU.mult, op1=ALU.mult))
                if ci % 2 == 0:
                    bank_free["B"] = (s_v.h, s_v.n)
                else:
                    psS_free = (s_v.h, s_v.n)
            Q_APPLIED_V = s_v.n
            T.wait_ge(s_s.h, KR_S)
            kb_i = 0
            for ci in range(2):
                for (c0, c1) in [(0, 512), (512, TT)]:
                    bank = "A" if kb_i % 2 == 0 else "B"
                    kb_i += 1
                    ps = psA if bank == "A" else psB
                    mm_gate(bank)
                    last = T.matmul(ps[0:128, 0:c1 - c0],
                                    selkT[0:NKV, 128 * ci: 128 * ci + 128],
                                    krow[0:NKV, c0:c1], start=True, stop=True)
                    s_t.inc(last)
                    V.wait_ge(s_t.h, s_t.n)
                    s_v.inc(V.scalar_tensor_tensor(
                        kbuf[:, TT * ci + c0: TT * ci + c1],
                        kbuf[:, TT * ci + c0: TT * ci + c1], knw[:, 0:1],
                        ps[0:128, 0:c1 - c0], op0=ALU.mult, op1=ALU.mult))
                    bank_free[bank] = (s_v.h, s_v.n)
            K_APPLIED_V = s_v.n
            emit_r_filler(1, "A")

            _chk(tt, "rms")
            ptbuf_evac_s = None
            pv_evac_v = None
            prev = None

            def emit_qk(ss, qoff, qlen, kw):
                nonlocal_vals = {}
                msk = logmf if (tt == 0 and ss == 0) else (logm if qlen == 112 else logm64)
                T.wait_ge(s_v.h, max(Q_APPLIED_V, K_APPLIED_V))
                if ptbuf_evac_s is not None:
                    T.wait_ge(s_s.h, ptbuf_evac_s)
                else:
                    T.wait_ge(psS_free[0], psS_free[1])
                    mm_gate("S0")
                    mm_gate("S1")
                for j in range(NH):
                    s = (j // 2) + 8 * (j % 2)
                    h = 2 * (s % 8) + s // 8
                    pq = 64 * (h % 2)
                    ciq = h // 2
                    cik = (h % 4) // 2
                    lhs = qbuf[pq:pq + 64, 512 * ciq + qoff: 512 * ciq + qoff + qlen]
                    rhs = kbuf[pq:pq + 64, TT * cik + qoff: TT * cik + qoff + kw]
                    last = T.matmul(psS[0:qlen, 128 * s: 128 * s + kw], lhs, rhs,
                                    start=True, stop=True)
                s_t.inc(last)
                return msk, s_t.n

            def emit_softmax(msk, qk_t, qlen, kw):
                V.wait_ge(s_t.h, qk_t)
                sin = psS[0:qlen, 0:NH * 128].rearrange(
                    "q (h k) -> q h k", h=NH)[:, :, 0:kw]
                min_ = msk[0:qlen, 0:NH * kw].rearrange("q (h k) -> q h k", h=NH)
                pout = pbuf[0:qlen, 0:NH * kw].rearrange("q (h k) -> q h k", h=NH)
                s_v.inc(V.scalar_tensor_tensor(pout, sin, C_SCALE(qlen), min_,
                                               op0=ALU.mult, op1=ALU.add))
                S.wait_ge(s_v.h, s_v.n)
                s_s.inc(S.activation(pbuf[0:qlen, 0:NH * kw],
                                     pbuf[0:qlen, 0:NH * kw], AF.Exp))
                V.wait_ge(s_s.h, s_s.n)
                s_v.inc(V.tensor_reduce(lbuf[0:qlen, 0:NH], pout, axis=AX.X,
                                        op=ALU.add))
                s_v.inc(V.reciprocal(lbuf[0:qlen, NH:2 * NH], lbuf[0:qlen, 0:NH]))
                rb = lbuf[0:qlen, NH:2 * NH].unsqueeze(-1).broadcast_to(
                    [qlen, NH, kw])
                s_v.inc(V.tensor_tensor(pout, pout, rb, op=ALU.mult))
                return s_v.n

            def emit_transposes(pnorm_v, qlen, kw):
                nonlocal ptbuf_evac_s
                T.wait_ge(s_v.h, pnorm_v)
                per_bank = 4 if qlen == 112 else 8
                nbanks = NH // per_bank
                bank_t = []
                for b in range(nbanks):
                    for j in range(per_bank):
                        s = b * per_bank + j
                        tin = pbuf[0:qlen, kw * s: kw * s + kw]
                        col = 512 * b + qlen * j
                        last = T.transpose(psS[0:kw, col: col + qlen], tin,
                                           ident[0:qlen, 0:qlen])
                    s_t.inc(last)
                    bank_t.append(s_t.n)
                for b in range(nbanks):
                    S.wait_ge(s_t.h, bank_t[b])
                    s_s.inc(S.copy(
                        ptbuf[0:kw, per_bank * qlen * b: per_bank * qlen * (b + 1)],
                        psS[0:kw, 512 * b: 512 * b + per_bank * qlen]))
                ptbuf_evac_s = s_s.n

            def emit_pv_oproj(ss, qoff, qlen, kw, ptevac):
                nonlocal pv_evac_v
                T.wait_ge(s_s.h, ptevac)
                if pv_evac_v is not None:
                    T.wait_ge(s_s.h, pv_evac_v)
                else:
                    T.wait_ge(psPV_free[0], psPV_free[1])
                for s in range(NH):
                    h = 2 * (s % 8) + s // 8
                    kv = h % 4
                    lhs = vbuf[0:kw, 256 * ss + 64 * kv: 256 * ss + 64 * kv + 64]
                    rhs = ptbuf[0:kw, qlen * s: qlen * s + qlen]
                    pp = 64 * (h % 2)
                    col = 128 * (h // 2)
                    last = T.matmul(psPV[pp:pp + 64, col: col + qlen], lhs, rhs,
                                    start=True, stop=True)
                s_t.inc(last)
                S.wait_ge(s_t.h, s_t.n)
                pvin = psPV[:, 0:1024].rearrange("p (c t) -> p c t", t=128)[:, :, 0:qlen]
                aout = asub[:, 0:NCH * 112].rearrange(
                    "p (c t) -> p c t", t=112)[:, :, 0:qlen]
                s_s.inc(S.copy(aout, pvin))
                pv_evac_v = s_s.n
                T.wait_ge(s_ws[wo_info[0][0] % 3].h, wo_info[0][1])
                T.wait_ge(s_ws[wo_info[1][0] % 3].h, wo_info[1][1])
                T.wait_ge(s_s.h, pv_evac_v)
                for co in range(NCH):
                    wo_v = wslot_view(wo_info[0][0] if co < 4 else wo_info[1][0])
                    bank = "A" if (co % 2 == 0) else "B"
                    ps = psA if bank == "A" else psB
                    mm_gate(bank)
                    for ci in range(NCH):
                        lhs = wo_v[:, 512 * ci + 128 * (co % 4):
                                   512 * ci + 128 * (co % 4) + 128]
                        last = T.matmul(ps[0:128, 0:qlen], lhs,
                                        asub[:, 112 * ci: 112 * ci + qlen],
                                        start=(ci == 0), stop=(ci == NCH - 1))
                    s_t.inc(last)
                    V.wait_ge(s_t.h, s_t.n)
                    xsl = xbuf[:, TT * co + HALO + qoff: TT * co + HALO + qoff + qlen]
                    ysl = xnbuf[:, TT * co + HALO + qoff: TT * co + HALO + qoff + qlen]
                    s_v.inc(V.scalar_tensor_tensor(ysl, ps[0:128, 0:qlen], C_ONE(128),
                                                   xsl, op0=ALU.mult, op1=ALU.add))
                    bank_free[bank] = (s_v.h, s_v.n)

            for ss, (qoff, qlen) in enumerate(SUBTILES):
                kw = qlen + 16
                msk, qk_t = emit_qk(ss, qoff, qlen, kw)
                pnorm_v = emit_softmax(msk, qk_t, qlen, kw)
                if prev is not None:
                    emit_pv_oproj(*prev)
                emit_transposes(pnorm_v, qlen, kw)
                prev = (ss, qoff, qlen, kw, ptbuf_evac_s)
            emit_pv_oproj(*prev)
            YADDS_V = s_v.n
            PSS_GATE = (s_s.h, ptbuf_evac_s)
            bank_free["S0"] = (s_s.h, ptbuf_evac_s)
            bank_free["S1"] = (s_s.h, ptbuf_evac_s)
            PSPV_GATE = (s_s.h, pv_evac_v)
            wslot_free[wo_info[0][0] % 3] = (s_t.h, s_t.n)
            wslot_free[wo_info[1][0] % 3] = (s_t.h, s_t.n)
            wr_info[1] = load_weight_half(wr3, 1)
            if tt + 1 < NT:
                SY.wait_ge(s_v.h, YADDS_V)
                s_x.inc(SY.dma_start(
                    out=xbuf.rearrange("p (c t) -> p c t", t=TT),
                    in_=xin3[:, :, TILE * (tt + 1): TILE * (tt + 1) + TT]), 16)
                X_READY = s_x.n

            _chk(tt, "attn")

            def ffn_phase(w_infos, rhs_buf, evac, stride=512, co0=0):
                for co in range(co0, NCH):
                    (wi, wc) = w_infos[0] if co < 4 else w_infos[1]
                    T.wait_ge(s_ws[wi % 3].h, wc)
                    wv_ = wslot_view(wi)
                    bank = ROT4[co % 4]
                    ps = bank_ps(bank)
                    mm_gate(bank)
                    for ci in range(NCH):
                        lhs = wv_[:, 512 * ci + 128 * (co % 4):
                                  512 * ci + 128 * (co % 4) + 128]
                        last = T.matmul(ps[0:128, 0:512], lhs,
                                        rhs_buf[:, stride * ci: stride * ci + 512],
                                        start=(ci == 0), stop=(ci == NCH - 1))
                    s_t.inc(last)
                    evac(co, ps, s_t.n, bank)

            _chk(tt, "ffnmix")
            T.wait_ge(s_v.h, XR_V)
            S.wait_ge(s_v.h, XK_V)

            def r_evac(co, ps, tcnt, bank):
                S.wait_ge(s_t.h, tcnt)
                s_s.inc(S.activation(qbuf[:, 512 * co: 512 * co + 512],
                                     ps[0:128, 0:512], AF.Sigmoid))
                bank_free[bank] = (s_s.h, s_s.n)
            ffn_phase(wr_info, xmix, r_evac, co0=2)
            wslot_free[wr_info[0][0] % 3] = (s_t.h, s_t.n)
            wslot_free[wr_info[1][0] % 3] = (s_t.h, s_t.n)
            wkf_info = [load_weight_half(wkf3, 0), load_weight_half(wkf3, 1)]
            _chk(tt, "ffnr")
            T.wait_ge(s_v.h, XK_V)

            def k_evac(co, ps, tcnt, bank):
                V.wait_ge(s_t.h, tcnt)
                s_v.inc(V.tensor_scalar_max(xmix[:, 512 * co: 512 * co + 512],
                                            ps[0:128, 0:512], C_ZERO(128)))
                bank_free[bank] = (s_v.h, s_v.n)
                S.wait_ge(s_v.h, s_v.n)
                s_s.inc(S.activation(xmix[:, 512 * co: 512 * co + 512],
                                     xmix[:, 512 * co: 512 * co + 512], AF.Square))
            ffn_phase(wkf_info, scr, k_evac, stride=TT)
            KMM_LAST = s_t.n
            wslot_free[wkf_info[0][0] % 3] = (s_t.h, s_t.n)
            wslot_free[wkf_info[1][0] % 3] = (s_t.h, s_t.n)
            wvf_info = [load_weight_half(wvf3, 0), load_weight_half(wvf3, 1)]
            _chk(tt, "ffnk")
            KACT_S = s_s.n
            T.wait_ge(s_s.h, KACT_S)

            gadd_n = {}

            def v_evac(co, ps, tcnt, bank):
                V.wait_ge(s_t.h, tcnt)
                tsl = pbuf[:, 512 * (co % 2): 512 * (co % 2) + 512]
                if co >= 2:
                    V.wait_ge(s_g.h, gadd_n[co - 2])
                rsrc = (rbuf2[:, 512 * co: 512 * co + 512] if co < 2
                        else qbuf[:, 512 * co: 512 * co + 512])
                s_v.inc(V.tensor_tensor(tsl, ps[0:128, 0:512], rsrc,
                                        op=ALU.mult))
                bank_free[bank] = (s_v.h, s_v.n)
                G.wait_ge(s_v.h, s_v.n)
                ysl = xnbuf[:, TT * co + HALO: TT * co + TT]
                s_g.inc(G.tensor_tensor(ysl, ysl, tsl, op=ALU.add))
                gadd_n[co] = s_g.n
            ffn_phase(wvf_info, xmix, v_evac)
            VMM_LAST = s_t.n
            VEVAC_LAST = s_v.n
            GADD_LAST = s_g.n
            wslot_free[wvf_info[0][0] % 3] = (s_t.h, s_t.n)
            wslot_free[wvf_info[1][0] % 3] = (s_t.h, s_t.n)
            if tt + 1 < NT:
                wkv_info = load_weight_half(None, 0, kvpair=True)
                wq_info = [load_weight_half(wq3, 0), None]

            SY.wait_ge(s_g.h, s_g.n)
            s_y.inc(SY.dma_start(
                out=yout3[:, :, TILE * tt: TILE * tt + TILE],
                in_=xnbuf.rearrange("p (c t) -> p c t", t=TT)[:, :, HALO:TT]), 16)
        except _StopBuild:
            SY.wait_ge(s_t.h, s_t.n)
            SY.wait_ge(s_v.h, s_v.n)
            SY.wait_ge(s_s.h, s_s.n)
            SY.wait_ge(s_g.h, s_g.n)
            done = s_y.n // 16
            for t2 in range(done, NT):
                s_y.inc(SY.dma_start(
                    out=yout3[:, :, TILE * t2: TILE * t2 + TILE],
                    in_=xbuf.rearrange("p (c t) -> p c t", t=TT)[:, :, HALO:TT]), 16)

        G.wait_ge(s_y.h, 16 * NT)

        block.sync(SY._flush)
        block.tensor(T._flush)
        block.vector(V._flush)
        block.scalar(S._flush)
        block.gpsimd(G._flush)

    return nc


_NC_CACHE = [None]


def kernel(L0_attn_x, L0_attn_wq, L0_attn_wk, L0_attn_wv, L0_attn_wo,
           L0_attn_q_norm_w, L0_attn_k_norm_w, L0_ffn_W_r, L0_ffn_W_k,
           L0_ffn_W_v, L0_ffn_mu_r, L0_ffn_mu_k):
    x = np.asarray(L0_attn_x, dtype=np.float32)
    f32c = lambda a: np.ascontiguousarray(np.asarray(a, dtype=np.float32))
    wq_t = f32c(np.asarray(L0_attn_wq).T)
    wk_t = f32c(np.asarray(L0_attn_wk).T)
    wv_t = f32c(np.asarray(L0_attn_wv).T)
    wo_t = f32c(np.asarray(L0_attn_wo).T)
    wr_t = f32c(np.asarray(L0_ffn_W_r).T)
    wkf_t = f32c(np.asarray(L0_ffn_W_k).T)
    wvf_t = f32c(np.asarray(L0_ffn_W_v).T)
    qnw = f32c(np.tile(np.asarray(L0_attn_q_norm_w), 2)[:, None])
    knw = f32c(np.tile(np.asarray(L0_attn_k_norm_w), 2)[:, None])
    mur = f32c(np.asarray(L0_ffn_mu_r).reshape(NCH, 128).T)
    muk = f32c(np.asarray(L0_ffn_mu_k).reshape(NCH, 128).T)
    scal = f32c(np.tile(np.array([0.125, 1e-5, 0.01, -1.0, 1.0, 0.0, 0.0, 0.0],
                                 dtype=np.float32), (128, 1)))
    cn = _consts()

    in_maps = []
    for c in range(8):
        b, h = c // 2, c % 2
        xt = x[b].T
        if h == 0:
            xcore = np.concatenate(
                [np.zeros((D, HALO), dtype=np.float32), xt[:, :CORE_T]], axis=1)
            lmf = cn["logmf"]
        else:
            xcore = xt[:, CORE_T - HALO: 2 * CORE_T]
            lmf = cn["logm"]
        in_maps.append({
            "xin": np.ascontiguousarray(xcore),
            "w_wq": wq_t, "w_wk": wk_t, "w_wv": wv_t, "w_wo": wo_t,
            "w_wr": wr_t, "w_wkf": wkf_t, "w_wvf": wvf_t,
            "c_ident": cn["ident"], "c_logm": cn["logm"], "c_logmf": lmf,
            "c_logm64": cn["logm64"], "c_selq": cn["selq"], "c_selqT": cn["selqT"],
            "c_selk": cn["selk"], "c_selkT": cn["selkT"],
            "c_qnw": qnw, "c_knw": knw, "c_mur": mur, "c_muk": muk, "c_scal": scal,
        })

    if _NC_CACHE[0] is None:
        _NC_CACHE[0] = build_program()
    nc = _NC_CACHE[0]
    res = run_bass_kernel_spmd(nc, in_maps, core_ids=list(range(8)))

    y = np.empty((B, L, D), dtype=np.float32)
    for c in range(8):
        b, h = c // 2, c % 2
        y[b, h * CORE_T:(h + 1) * CORE_T, :] = res.results[c]["yout"].T
    return y
